# version 02
# baseline (speedup 1.0000x reference)
import contextlib
import numpy as np
import concourse.bass as bass
import concourse.mybir as mybir
from concourse.bass_utils import run_bass_kernel_spmd

F32 = mybir.dt.float32
BF16 = mybir.dt.bfloat16
U32 = mybir.dt.uint32
AF = mybir.ActivationFunctionType
ALU = mybir.AluOpType

D = 2048
HD = 128
NCORES = 8
INTER = 5632
RANK = 16
SCALE = float(HD) ** -0.5
EPS = 1e-5
DCH = D // 128
ICH = INTER // 128
ISTR = INTER // 512

ENGS = ("sync", "gpsimd", "tensor", "scalar", "vector")


class Sem:
    __slots__ = ("h", "n")

    def __init__(self, h):
        self.h = h
        self.n = 0


class KB:
    def __init__(self, nc):
        self.nc = nc
        self.ops = {e: [] for e in ENGS}
        self.sems = {}

    def sem(self, name):
        s = self.sems.get(name)
        if s is None:
            s = self.sems[name] = Sem(self.nc.alloc_semaphore(name))
        return s

    def emit(self, eng, fn, waits=(), inc=None, n=1):
        wl = [(s.h, v) for (s, v) in waits if v > 0]
        ih, inn = (inc.h, n) if inc is not None else (None, 0)
        if inc is not None:
            inc.n += n

        def run(e, fn=fn, wl=wl, ih=ih, inn=inn):
            for h, v in wl:
                e.wait_ge(h, v)
            if fn is not None:
                ins = fn(e)
                if ih is not None:
                    ins.then_inc(ih, inn)
        self.ops[eng].append(run)


def _build(T):
    OWN = T // NCORES
    NKT = T // 128
    NSTRIP = T // 512
    OTS = OWN // 128
    NSEG = T // 512
    NGRP = NKT // 2
    OBLK = 4
    assert OWN % 512 == 0

    nc = bass.Bass("TRN2", target_bir_lowering=False)

    x_in = nc.dram_tensor("x", [T, D], F32, kind="ExternalInput")
    x_own = nc.dram_tensor("x_own", [OWN, D], F32, kind="ExternalInput")
    wT = {p: nc.dram_tensor(f"w{p}T", [D, 256], F32, kind="ExternalInput")
          for p in ("q", "k", "v")}
    woT = nc.dram_tensor("woT", [D, D], F32, kind="ExternalInput")
    bqkv = nc.dram_tensor("bqkv", [128, 6], F32, kind="ExternalInput")
    bvrow = nc.dram_tensor("bvrow", [1, 256], F32, kind="ExternalInput")
    borow = nc.dram_tensor("borow", [1, D], F32, kind="ExternalInput")
    aT = {p: nc.dram_tensor(f"a{p}T", [RANK, 256], F32, kind="ExternalInput")
          for p in ("q", "k", "v")}
    aoT_d = nc.dram_tensor("aoT", [RANK, D], F32, kind="ExternalInput")
    bm = {p: nc.dram_tensor(f"b{p}m", [RANK, D], F32, kind="ExternalInput")
          for p in ("q", "k", "v")}
    bom_d = nc.dram_tensor("bom", [RANK, D], F32, kind="ExternalInput")
    gateT = nc.dram_tensor("gateT", [D, INTER], F32, kind="ExternalInput")
    upT = nc.dram_tensor("upT", [D, INTER], F32, kind="ExternalInput")
    down_in = nc.dram_tensor("down_w", [D, INTER], F32, kind="ExternalInput")
    gates_in = nc.dram_tensor("gates", [128, ICH], F32, kind="ExternalInput")
    ups_in = nc.dram_tensor("ups", [128, ICH], F32, kind="ExternalInput")
    downs_in = nc.dram_tensor("downs", [128, DCH], F32, kind="ExternalInput")
    coff = nc.dram_tensor("coff", [1, 1], U32, kind="ExternalInput")

    out_t = nc.dram_tensor("out", [OWN, D], F32, kind="ExternalOutput")

    down_bf = nc.dram_tensor("down_bf", [D, INTER], BF16)
    cc_in = nc.dram_tensor("cc_in", [T, 256], BF16)
    cc_out = nc.dram_tensor("cc_out", [NCORES * T, 256], BF16)

    kb = KB(nc)
    S = kb.sem
    A = nc.alloc_sbuf_tensor

    ones1 = A("ones1", [1, 128], BF16)
    onescol = A("onescol", [128, 1], BF16)
    ident = A("ident", [128, 128], BF16)
    borow_sb = A("borow_sb", [1, D], BF16)
    gates_sb = A("gates_sb", [128, ICH], F32)
    ups_sb = A("ups_sb", [128, ICH], F32)
    downs_sb = A("downs_sb", [128, DCH], F32)
    coff_sb = A("coff_sb", [1, 1], U32)
    negM = A("negM", [128, 2], F32)
    sbEps = A("sbEps", [128, 1], F32)
    sbZero = A("sbZero", [128, 1], F32)

    sv, ss, st, sg, scc = S("sv"), S("ss"), S("st"), S("sg"), S("scc")
    sSmall, sCastW, sCastB, sCastD, sScr = (S("sSmall"), S("sCastW"), S("sCastB"), S("sCastD"), S("sScr"))
    sXa, sXb = S("sXa"), S("sXb")
    sXsem = [sXa, sXb]
    sCCs, sDyn, sWoA = S("sCCs"), S("sDyn"), S("sWoA")
    sWoW = [S("sWoWa"), S("sWoWb")]
    sWoB = [S("sWoBa"), S("sWoBb")]
    sDUp = [S("sDUa"), S("sDUb")]
    sXOp = [S("sXOa"), S("sXOb")]
    sOutp = [S("sOuta"), S("sOutb")]
    sGUs = [S("sGUa"), S("sGUb")]

    kb.emit("gpsimd", lambda e: e.memset(ones1[:], 1.0), inc=sg)
    kb.emit("gpsimd", lambda e: e.memset(onescol[:], 1.0), inc=sg)
    kb.emit("gpsimd", lambda e: e.memset(ident[:], 0.0), inc=sg)
    kb.emit("gpsimd", lambda e: e.memset(sbEps[:], EPS), inc=sg)
    kb.emit("gpsimd", lambda e: e.memset(sbZero[:], 0.0), inc=sg)
    kb.emit("gpsimd", lambda e: e.affine_select(
        out=ident[:], in_=ident[:], compare_op=ALU.not_equal, fill=1.0,
        base=0, pattern=[[-1, 128]], channel_multiplier=1),
        waits=[(sg, 3)], inc=sg)
    SG_CONST = sg.n

    for (dst, src) in ((gates_sb, gates_in), (ups_sb, ups_in),
                       (downs_sb, downs_in), (coff_sb, coff)):
        kb.emit("gpsimd", lambda e, d=dst, s=src: e.dma_start(out=d[:], in_=s[:]),
                inc=sSmall, n=16)
    kb.emit("gpsimd", lambda e: e.dma_start(out=borow_sb[:], in_=borow[:],
            max_dma_last_dim=4096), inc=sSmall, n=16)

    scA = contextlib.ExitStack()
    AA = lambda name, shape, dt: scA.enter_context(nc.sbuf_tensor(name, shape, dt))

    wt = {p: AA(f"wt_{p}", [128, DCH * 256], BF16) for p in ("q", "k", "v")}
    xnf = AA("xnf", [128, DCH * 512], BF16)
    QT = [AA(f"QT{h}", [128, T], BF16) for h in range(2)]
    KT = [AA(f"KT{h}", [128, T], BF16) for h in range(2)]
    VA = [AA(f"VA{h}", [128, NKT * 129], BF16) for h in range(2)]
    sbP = [AA(f"sbP{i}", [128, 1024], BF16) for i in range(2)]
    bqkv_sb = AA("bqkv_sb", [128, 6], F32)
    bvrow_sb = AA("bvrow_sb", [1, 256], BF16)
    o_sb = AA("o_sb", [128, 512], BF16)
    sq_sb = [AA(f"sq_sb{i}", [128, 512], BF16) for i in range(2)]
    sbRec = AA("sbRec", [128, 4], F32)
    sbMseg = AA("sbMseg", [1, 1], F32)
    sbRunQ = AA("sbRunQ", [1, 1], F32)
    sbRunK = AA("sbRunK", [1, 1], F32)
    sbProd = AA("sbProd", [1, 1], F32)
    sbMx = AA("sbMx", [1, 1], F32)
    sbMbf = AA("sbMbf", [1, 1], BF16)
    lnA = {
        "x": [AA(f"lnx{i}", [128, D], F32) for i in range(2)],
        "xn": [AA(f"lnxn{i}", [128, D], BF16) for i in range(2)],
        "bn6": AA("lnbn6", [128, 24], F32),
        "mv": AA("lnmv", [128, 2], F32),
        "std": AA("lnstd", [128, 1], F32),
        "rinv": AA("lnrinv", [128, 1], F32),
        "nmr": AA("lnnmr", [128, 1], F32),
    }

    kb.emit("gpsimd", lambda e: e.dma_start(out=bqkv_sb[:], in_=bqkv[:]),
            inc=sSmall, n=16)
    kb.emit("gpsimd", lambda e: e.dma_start(out=bvrow_sb[:], in_=bvrow[:],
            max_dma_last_dim=4096), inc=sSmall, n=16)
    SMALL_ALL = sSmall.n
    SMALL_A = sSmall.n

    for h in range(2):
        kb.emit("gpsimd", lambda e, h=h: e.memset(
            VA[h][:].rearrange("p (k d) -> p k d", d=129)[:, :, 128:129], 1.0), inc=sg)
    VA_ONES = sg.n

    scA0 = contextlib.ExitStack()
    A0 = lambda name, shape, dt: scA0.enter_context(nc.sbuf_tensor(name, shape, dt))
    sbATl = {p: A0(f"sbAT_{p}", [RANK, 256], BF16) for p in ("q", "k", "v")}
    sbBm = A0("sbBm", [RANK, D], BF16)
    wnat = [A0(f"wnat{i}", [128, 2816], BF16) for i in range(2)]
    psm1 = contextlib.ExitStack()
    ps_T = psm1.enter_context(nc.psum_tensor("ps_T", [128, 2048], BF16))
    ps_qk = [psm1.enter_context(nc.psum_tensor(f"ps_qk{i}", [128, 512], F32))
             for i in range(2)]
    ps_v = psm1.enter_context(nc.psum_tensor("ps_v", [128, 256], F32))
    ps_ab = psm1.enter_context(nc.psum_tensor("ps_ab", [128, 256], F32))

    for p in ("q", "k", "v"):
        kb.emit("gpsimd", lambda e, d=wt[p], s=wT[p]: e.dma_start(
            out=d[:].rearrange("p (c o) -> p c o", o=256),
            in_=s[:].rearrange("(c p) o -> p c o", p=128),
            max_dma_last_dim=4096), inc=sCastW, n=16)
        kb.emit("gpsimd", lambda e, d=sbATl[p], s=aT[p]: e.dma_start(
            out=d[:], in_=s[:], max_dma_last_dim=4096), inc=sCastW, n=16)
    CAST_W = sCastW.n

    fold_v_prev = 0
    for ip, p in enumerate(("q", "k", "v")):
        kb.emit("gpsimd", lambda e, s=bm[p]: e.dma_start(
            out=sbBm[:], in_=s[:], max_dma_last_dim=4096),
            waits=[(sv, fold_v_prev)], inc=sCastB, n=16)
        bm_ready = sCastB.n
        for c in range(DCH):
            kb.emit("tensor", lambda e, p=p, c=c: e.matmul(
                ps_ab[:], sbBm[:, c * 128:(c + 1) * 128], sbATl[p][:],
                start=True, stop=True),
                waits=[(sCastW, CAST_W), (sCastB, bm_ready), (sv, sv.n)],
                inc=st)
            kb.emit("vector", lambda e, p=p, c=c: e.scalar_tensor_tensor(
                wt[p][:, c * 256:(c + 1) * 256], ps_ab[:], 0.5,
                wt[p][:, c * 256:(c + 1) * 256], op0=ALU.mult, op1=ALU.add),
                waits=[(st, st.n)], inc=sv)
        fold_v_prev = sv.n
    WT_READY = sv.n

    scr_rounds = []
    for r in range(DCH):
        for half in range(2):
            buf = wnat[(r * 2 + half) % 2]
            c0 = half * 2816
            kb.emit("gpsimd", lambda e, b=buf, r=r, c0=c0: e.dma_start(
                out=b[:], in_=down_in[r * 128:(r + 1) * 128, c0:c0 + 2816],
                max_dma_last_dim=4096),
                waits=[(sScr, sScr.n), (sSmall, SMALL_ALL)], inc=sCastD, n=16)
            kb.emit("vector", lambda e, b=buf, r=r: e.tensor_scalar_mul(
                b[:], b[:], downs_sb[:, r:r + 1]),
                waits=[(sCastD, sCastD.n)], inc=sv)
            kb.emit("gpsimd", lambda e, b=buf, r=r, c0=c0: e.dma_start(
                out=down_bf[r * 128:(r + 1) * 128, c0:c0 + 2816], in_=b[:]),
                waits=[(sv, sv.n)], inc=sScr, n=16)
            scr_rounds.append(sScr.n)
    SCR_ALL = sScr.n


    ln_state = {"apply": [0, 0], "tr": [0, 0], "evac": 0}

    def ln_tile(src_fn, ti, ln, extra_x_waits=()):
        b = ti % 2
        kb.emit("sync", lambda e, s=src_fn: e.dma_start(out=ln["x"][b][:], in_=s()),
                waits=[(sv, ln_state["apply"][b])] + list(extra_x_waits),
                inc=sXsem[b], n=16)
        xw = (sXsem[b], sXsem[b].n)
        for k in range(4):
            kb.emit("vector", lambda e, k=k, b=b, ln=ln: e.bn_stats(
                ln["bn6"][:, 6 * k:6 * k + 6], ln["x"][b][:, 512 * k:512 * (k + 1)]),
                waits=([xw, (sv, sv.n), (ss, ss.n)] if k == 0 else []),
                inc=(sv if k == 3 else None))
        kb.emit("vector", lambda e, ln=ln: e.bn_aggr(ln["mv"][:], ln["bn6"][:]),
                waits=[(sv, sv.n)], inc=sv)
        kb.emit("scalar", lambda e, ln=ln: e.activation(
            ln["std"][:], ln["mv"][:, 1:2], AF.Sqrt, bias=sbEps[:]),
            waits=[(sv, sv.n), (sg, SG_CONST)], inc=ss)
        kb.emit("vector", lambda e, ln=ln: e.reciprocal(ln["rinv"][:], ln["std"][:]),
                waits=[(ss, ss.n)], inc=sv)
        kb.emit("vector", lambda e, ln=ln: e.scalar_tensor_tensor(
            ln["nmr"][:], ln["mv"][:, 0:1], -1.0, ln["rinv"][:],
            op0=ALU.mult, op1=ALU.mult),
            waits=[(sv, sv.n)], inc=sv)
        kb.emit("vector", lambda e, b=b, ln=ln: e.tensor_scalar(
            ln["xn"][b][:], ln["x"][b][:], ln["rinv"][:], ln["nmr"][:],
            op0=ALU.mult, op1=ALU.add),
            waits=[(sv, sv.n), (st, ln_state["tr"][b])], inc=sv)
        ln_state["apply"][b] = sv.n
        return sv.n

    def xn_transpose(ln, b, xn_ready, dst_ap_fn, war_st, psT):
        for c in range(DCH):
            kb.emit("tensor", lambda e, c=c, b=b, ln=ln, psT=psT: e.transpose(
                psT[:, 128 * c:128 * (c + 1)], ln["xn"][b][:, 128 * c:128 * (c + 1)],
                ident[:]),
                waits=([(sv, xn_ready), (sg, SG_CONST), (sv, ln_state["evac"])]
                       if c == 0 else []),
                inc=(st if c == DCH - 1 else None))
        trm = st.n
        ln_state["tr"][b] = trm
        kb.emit("vector", lambda e, psT=psT: e.tensor_copy(
            dst_ap_fn(), psT[:].rearrange("p (c d) -> p c d", d=128)),
            waits=[(st, trm), (st, war_st)], inc=sv)
        ln_state["evac"] = sv.n
        return sv.n

    qkv_after_strip = 0
    v_evac_last = 0
    for s_ in range(NSTRIP):
        for tl in range(4):
            ti = s_ * 4 + tl
            xn_ready = ln_tile(
                lambda ti=ti: x_in[ti * 128:(ti + 1) * 128, :], ti, lnA)
            xn_transpose(
                lnA, ti % 2, xn_ready,
                lambda tl=tl: xnf[:].rearrange(
                    "p (c t d) -> p c t d", t=4, d=128)[:, :, tl, :],
                qkv_after_strip, ps_T)
        strip_xnf = sv.n

        for p in ("q", "k"):
            for a in range(2):
                pq = ps_qk[a]
                for c in range(DCH):
                    w = ([(sv, strip_xnf), (sv, WT_READY), (ss, ss.n)]
                         if c == 0 else [])
                    kb.emit("tensor", lambda e, p=p, a=a, c=c, pq=pq: e.matmul(
                        pq[:], wt[p][:, c * 256 + a * 128: c * 256 + a * 128 + 128],
                        xnf[:, c * 512:(c + 1) * 512],
                        start=(c == 0), stop=True, skip_group_check=True),
                        waits=w, inc=(st if c == DCH - 1 else None))
                dst = QT[a] if p == "q" else KT[a]
                bc = (0 if p == "q" else 2) + a
                kb.emit("scalar", lambda e, d=dst, s_=s_, pq=pq, bc=bc: e.activation(
                    d[:, s_ * 512:(s_ + 1) * 512], pq[:], AF.Identity,
                    bias=bqkv_sb[:, bc:bc + 1], scale=1.0),
                    waits=[(st, st.n), (sSmall, SMALL_A)], inc=ss)
        for tl in range(4):
            kt = s_ * 4 + tl
            kb.emit("tensor", lambda e: e.matmul(
                ps_v[:], ones1[:], bvrow_sb[:], start=True, stop=True),
                waits=[(sv, v_evac_last), (sg, SG_CONST), (sSmall, SMALL_A),
                       (sv, strip_xnf)])
            for c in range(DCH):
                kb.emit("tensor", lambda e, tl=tl, c=c: e.matmul(
                    ps_v[:], xnf[:, c * 512 + tl * 128: c * 512 + tl * 128 + 128],
                    wt["v"][:, c * 256:(c + 1) * 256],
                    start=False, stop=True, skip_group_check=True),
                    inc=(st if c == DCH - 1 else None))
            for h in range(2):
                kb.emit("vector", lambda e, h=h, kt=kt: e.tensor_copy(
                    VA[h][:, 129 * kt:129 * kt + 128], ps_v[:, h * 128:(h + 1) * 128]),
                    waits=([(st, st.n)] if h == 0 else []),
                    inc=(sv if h == 1 else None))
            v_evac_last = sv.n
        qkv_after_strip = st.n
    P1_V_ALL = sv.n
    P1_QK_ALL = ss.n
    scA0.close()
    psm1.close()

    scN = contextlib.ExitStack()
    ps_n = [scN.enter_context(nc.psum_tensor(f"ps_n{i}", [1, 512], F32))
            for i in range(2)]
    ps_nm = scN.enter_context(nc.psum_tensor("ps_nm", [128, 1], F32))

    for h in range(2):
        kb.emit("gpsimd", lambda e: e.memset(sbRunQ[:], 0.0),
                waits=[(sv, sv.n)], inc=sg)
        kb.emit("gpsimd", lambda e: e.memset(sbRunK[:], 0.0), inc=sg)
        run_init = sg.n
        for (mat, run) in ((QT[h], sbRunQ), (KT[h], sbRunK)):
            for seg in range(NSEG):
                b = seg % 2
                kb.emit("scalar", lambda e, m=mat, seg=seg, b=b: e.activation(
                    sq_sb[b][:], m[:, seg * 512:(seg + 1) * 512], AF.Square,
                    bias=sbZero[:]),
                    waits=[(ss, P1_QK_ALL), (st, st.n)], inc=ss)
                kb.emit("tensor", lambda e, b=b: e.matmul(
                    ps_n[b][:], onescol[:], sq_sb[b][:], start=True, stop=True),
                    waits=[(ss, ss.n), (sv, sv.n), (sg, SG_CONST)], inc=st)
                kb.emit("vector", lambda e, b=b: e.reduce_max(
                    sbMseg[:], ps_n[b][:], axis=mybir.AxisListType.X),
                    waits=[(st, st.n), (sg, run_init), (sv, sv.n)], inc=sv)
                kb.emit("vector", lambda e, r=run: e.tensor_max(r[:], r[:], sbMseg[:]),
                        waits=[(sv, sv.n)], inc=sv)
        kb.emit("vector", lambda e: e.tensor_mul(sbProd[:], sbRunQ[:], sbRunK[:]),
                waits=[(sv, sv.n)], inc=sv)
        kb.emit("scalar", lambda e: e.activation(
            sbMx[:], sbProd[:], AF.Sqrt, bias=sbZero[0:1, :]),
                waits=[(sv, sv.n)], inc=ss)
        kb.emit("vector", lambda e: e.tensor_scalar_mul(sbMbf[:], sbMx[:], -SCALE),
                waits=[(ss, ss.n)], inc=sv)
        kb.emit("tensor", lambda e: e.matmul(
            ps_nm[:], ones1[:], sbMbf[:], start=True, stop=True),
            waits=[(sv, sv.n)], inc=st)
        kb.emit("vector", lambda e, h=h: e.tensor_copy(negM[:, h:h + 1], ps_nm[:]),
                waits=[(st, st.n)], inc=sv)
    NEGM_READY = sv.n
    scN.close()

    scAT = contextlib.ExitStack()
    ps_s = [scAT.enter_context(nc.psum_tensor(f"ps_s{i}", [128, 1024], F32))
            for i in range(2)]
    ps_oa = scAT.enter_context(nc.psum_tensor("ps_oa", [128, 512], F32))
    ps_ob = scAT.enter_context(nc.psum_tensor("ps_ob", [128, 512], F32))

    prev_exp = [0, 0]
    prev_av = [0, 0]
    norm_done = 0
    for h in range(2):
        for qs in range(NSTRIP):
            for g in range(NGRP):
                par = g % 2
                for kl in range(2):
                    kt = 2 * g + kl
                    w = []
                    if kl == 0:
                        w = [(ss, prev_exp[par])]
                        if g == 0 and qs == 0 and h == 0:
                            w += [(ss, P1_QK_ALL), (sv, NEGM_READY)]
                    kb.emit("tensor", lambda e, h=h, qs=qs, kt=kt, par=par, kl=kl:
                            e.matmul(
                        ps_s[par][:, kl * 512:(kl + 1) * 512],
                        KT[h][:, kt * 128:(kt + 1) * 128],
                        QT[h][:, qs * 512:(qs + 1) * 512],
                        start=True, stop=True, skip_group_check=True),
                        waits=w, inc=(st if kl == 1 else None))
                s_ms = st.n
                wex = [(st, s_ms), (st, prev_av[par])]
                if h == 0 and qs == 0 and g == 0:
                    wex.append((sv, NEGM_READY))
                kb.emit("scalar", lambda e, h=h, par=par: e.activation(
                    sbP[par][:], ps_s[par][:], AF.Exp,
                    bias=negM[:, h:h + 1], scale=SCALE),
                    waits=wex, inc=ss)
                prev_exp[par] = ss.n
                wav = [(ss, prev_exp[par])]
                if g == 0:
                    wav += [(sv, P1_V_ALL), (sg, VA_ONES), (sv, norm_done)]
                first = True
                for kl in range(2):
                    kt = 2 * g + kl
                    for j in range(4):
                        stt = (g == 0 and kl == 0 and j in (0, 3))
                        kb.emit("tensor", lambda e, h=h, kt=kt, kl=kl, j=j, par=par,
                                stt=stt: e.matmul(
                            (ps_oa[:, 129 * j:129 * j + 129] if j < 3
                             else ps_ob[:, 0:129]),
                            sbP[par][:, kl * 512 + j * 128: kl * 512 + j * 128 + 128],
                            VA[h][:, 129 * kt:129 * kt + 129],
                            start=stt, stop=True, skip_group_check=True),
                            waits=(wav if first else []),
                            inc=(st if (kl == 1 and j == 3) else None))
                        first = False
                prev_av[par] = st.n
            av_all = st.n
            for j in range(4):
                kb.emit("vector", lambda e, j=j: e.reciprocal(
                    sbRec[:, j:j + 1],
                    (ps_oa[:, 129 * j + 128:129 * j + 129] if j < 3
                     else ps_ob[:, 128:129])),
                    waits=([(st, av_all), (sCCs, sCCs.n), (sv, sv.n)]
                           if j == 0 else []),
                    inc=(sv if j == 3 else None))
            rcp = sv.n
            for j in range(4):
                kb.emit("vector", lambda e, j=j: e.tensor_scalar_mul(
                    o_sb[:, j * 128:(j + 1) * 128],
                    (ps_oa[:, 129 * j:129 * j + 128] if j < 3 else ps_ob[:, 0:128]),
                    sbRec[:, j:j + 1]),
                    waits=([(sv, rcp)] if j == 0 else []),
                    inc=(sv if j == 3 else None))
            norm_done = sv.n
            kb.emit("sync", lambda e, h=h, qs=qs: e.dma_start(
                out=cc_in[qs * 512:(qs + 1) * 512, h * 128:(h + 1) * 128]
                    .rearrange("(a p) d -> p a d", p=128),
                in_=o_sb[:].rearrange("p (a d) -> p a d", d=128)),
                waits=[(sv, norm_done)], inc=sCCs, n=16)
    CC_ALL = sCCs.n
    scAT.close()
    scA.close()

    kb.emit("gpsimd", lambda e: e.collective_compute(
        "AllGather", ALU.bypass, replica_groups=[list(range(NCORES))],
        ins=[cc_in[:].opt()], outs=[cc_out[:].opt()]),
        waits=[(sCCs, CC_ALL)], inc=scc)

    scB = contextlib.ExitStack()
    hdn = scB.enter_context(nc.sbuf_tensor("hdn", [128, ICH * OWN], BF16))

    scB1 = contextlib.ExitStack()
    B1 = lambda name, shape, dt: scB1.enter_context(nc.sbuf_tensor(name, shape, dt))
    xnf_own = B1("xnf_own", [128, DCH * OWN], BF16)
    sbStrip = [B1(f"sbStrip{i}", [128, DCH * 512], BF16) for i in range(2)]
    sbSilu = B1("sbSilu", [128, 4 * OWN], BF16)
    lnB = {
        "x": [B1(f"lnBx{i}", [128, D], F32) for i in range(2)],
        "xn": [B1(f"lnBxn{i}", [128, D], BF16) for i in range(2)],
        "bn6": B1("lnBbn6", [128, 24], F32),
        "mv": B1("lnBmv", [128, 2], F32),
        "std": B1("lnBstd", [128, 1], F32),
        "rinv": B1("lnBrinv", [128, 1], F32),
        "nmr": B1("lnBnmr", [128, 1], F32),
    }
    scB1p = contextlib.ExitStack()
    ps_T4 = scB1p.enter_context(nc.psum_tensor("ps_T4", [128, 2048], BF16))
    ps_g = scB1p.enter_context(nc.psum_tensor("ps_g", [128, OWN], F32))
    ps_u = scB1p.enter_context(nc.psum_tensor("ps_u", [128, OWN], F32))
    assert OWN <= 1024

    ln_state["tr"] = [0, 0]
    for ti in range(OTS):
        xn_ready = ln_tile(
            lambda ti=ti: x_own[ti * 128:(ti + 1) * 128, :], ti, lnB,
            extra_x_waits=[(sCCs, CC_ALL), (sScr, SCR_ALL)])
        xn_transpose(
            lnB, ti % 2, xn_ready,
            lambda ti=ti: xnf_own[:].rearrange(
                "p (c t d) -> p c t d", t=OTS, d=128)[:, :, ti, :],
            0, ps_T4)
    XNF_OWN = sv.n

    silu_ms = 0
    hdn_ms = 0
    gu_after = [0, 0]
    NB = OWN // 512
    for istr in range(ISTR):
        for iw in range(2):
            wsrc = gateT if iw == 0 else upT
            psx = ps_g if iw == 0 else ps_u
            buf = sbStrip[iw]
            wgu = [(st, gu_after[iw])]
            if istr == 0:
                wgu += [(sScr, SCR_ALL), (sCCs, CC_ALL)]
            kb.emit("gpsimd", lambda e, b=buf, s=wsrc, istr=istr: e.dma_start(
                out=b[:].rearrange("p (c i) -> p c i", i=512),
                in_=s[:, istr * 512:(istr + 1) * 512].rearrange("(c p) i -> p c i", p=128),
                max_dma_last_dim=4096),
                waits=wgu, inc=sGUs[iw], n=16)
            gu_ready = sGUs[iw].n
            for it in range(4):
                ich = istr * 4 + it
                for c in range(DCH):
                    for nb in range(NB):
                        w = []
                        if c == 0 and nb == 0:
                            w = [(sGUs[iw], gu_ready), (sv, XNF_OWN),
                                 (sCCs, CC_ALL)]
                            w.append((ss, silu_ms) if iw == 0 else (sv, hdn_ms))
                        kb.emit("tensor", lambda e, b=buf, c=c, it=it, nb=nb, psx=psx:
                                e.matmul(
                            psx[:, nb * 512:(nb + 1) * 512],
                            b[:, c * 512 + it * 128: c * 512 + it * 128 + 128],
                            xnf_own[:, c * OWN + nb * 512: c * OWN + nb * 512 + 512],
                            start=(c == 0), stop=True, skip_group_check=True),
                            waits=w,
                            inc=(st if (c == DCH - 1 and nb == NB - 1) else None))
                mm_ms = st.n
                if iw == 0:
                    kb.emit("scalar", lambda e, ich=ich, it=it: e.activation(
                        sbSilu[:, it * OWN:(it + 1) * OWN], ps_g[:], AF.Silu,
                        bias=sbZero[:], scale=gates_sb[:, ich:ich + 1]),
                        waits=[(st, mm_ms), (sv, hdn_ms), (sSmall, SMALL_ALL)],
                        inc=ss)
                    silu_ms = ss.n
                else:
                    kb.emit("vector", lambda e, ich=ich, it=it: e.scalar_tensor_tensor(
                        hdn[:, ich * OWN:(ich + 1) * OWN], ps_u[:],
                        ups_sb[:, ich:ich + 1], sbSilu[:, it * OWN:(it + 1) * OWN],
                        op0=ALU.mult, op1=ALU.mult),
                        waits=[(st, mm_ms), (ss, silu_ms)], inc=sv)
                    hdn_ms = sv.n
            gu_after[iw] = st.n
    HDN_ALL = sv.n
    scB1.close()

    scB2 = contextlib.ExitStack()
    B2 = lambda name, shape, dt: scB2.enter_context(nc.sbuf_tensor(name, shape, dt))
    sbWOT = B2("sbWOT", [128, DCH * 2048], BF16)
    sbAF = B2("sbAF", [128, DCH * OWN], BF16)
    sbATok = B2("sbATok", [128, OTS * 256], BF16)
    sbAoT = B2("sbAoT", [RANK, D], BF16)
    sbBo2 = [B2(f"sbBo2_{i}", [RANK, 128], BF16) for i in range(2)]
    sbDU = [B2(f"sbDU{i}", [128, 512], BF16) for i in range(2)]
    sbXO = [B2(f"sbXO{i}", [128, 512], F32) for i in range(2)]
    sbOut = [B2(f"sbOut{i}", [128, 512], F32) for i in range(2)]
    scB2a = contextlib.ExitStack()
    ps_ab4 = scB2a.enter_context(nc.psum_tensor("ps_ab4", [128, 256], F32))

    kb.emit("gpsimd", lambda e: e.dma_start(out=sbAoT[:], in_=aoT_d[:],
            max_dma_last_dim=4096), waits=[(sv, HDN_ALL)], inc=sWoA, n=16)
    fold_st = [0] * DCH
    for c in range(DCH):
        par = c % 2
        kb.emit("gpsimd", lambda e, c=c: e.dma_start(
            out=sbWOT[:, c * 2048:(c + 1) * 2048],
            in_=woT[c * 128:(c + 1) * 128, :], max_dma_last_dim=4096),
            waits=([(st, fold_st[c - 2])] if c >= 2 else []),
            inc=sWoW[par], n=16)
        kb.emit("gpsimd", lambda e, c=c, par=par: e.dma_start(
            out=sbBo2[par][:], in_=bom_d[:, c * 128:(c + 1) * 128],
            max_dma_last_dim=4096),
            waits=([(st, fold_st[c - 1])] if c >= 1 else []),
            inc=sWoB[par], n=16)
        wcnt = sWoW[par].n
        bcnt = sWoB[par].n
        for sgm in range(8):
            w = [(sv, sv.n)]
            if sgm == 0:
                w += [(sWoW[par], wcnt), (sWoB[par], bcnt), (sWoA, 16),
                      (sCCs, CC_ALL)]
            kb.emit("tensor", lambda e, c=c, sgm=sgm, par=par: e.matmul(
                ps_ab4[:], sbBo2[par][:], sbAoT[:, sgm * 256:(sgm + 1) * 256],
                start=True, stop=True),
                waits=w, inc=st)
            kb.emit("vector", lambda e, c=c, sgm=sgm: e.scalar_tensor_tensor(
                sbWOT[:, c * 2048 + sgm * 256: c * 2048 + (sgm + 1) * 256],
                ps_ab4[:], 0.5,
                sbWOT[:, c * 2048 + sgm * 256: c * 2048 + (sgm + 1) * 256],
                op0=ALU.mult, op1=ALU.add),
                waits=[(st, st.n)], inc=sv)
        fold_st[c] = st.n
    WO_FOLD = sv.n

    atok_tr = 0
    for r in range(NCORES):
        kb.emit("gpsimd", lambda e, r=r: (lambda offv: e.dma_start(
            out=sbATok[:].rearrange("p (a d) -> p a d", d=256),
            in_=cc_out[bass.ds(offv + r * T, OWN), :]
                .rearrange("(a p) d -> p a d", p=128)))(
            e.value_load(coff_sb[0:1, 0:1])),
            waits=[(scc, 1), (sSmall, SMALL_ALL), (st, atok_tr)] +
                  ([(sv, HDN_ALL)] if r == 0 else []), inc=sDyn, n=16)
        nblk = OTS * 2
        for ts in range(OTS):
            for dc2 in range(2):
                bi = ts * 2 + dc2
                kb.emit("tensor", lambda e, ts=ts, dc2=dc2, bi=bi: e.transpose(
                    ps_T4[:, 128 * bi:128 * (bi + 1)],
                    sbATok[:, ts * 256 + dc2 * 128: ts * 256 + dc2 * 128 + 128],
                    ident[:]),
                    waits=([(sDyn, sDyn.n), (sv, sv.n)]
                           if (ts == 0 and dc2 == 0) else []),
                    inc=(st if (ts * 2 + dc2 + 1) == nblk else None))
        trm = st.n
        kb.emit("vector", lambda e, r=r: e.tensor_copy(
            sbAF[:, r * 2 * OWN:(r + 1) * 2 * OWN]
                .rearrange("p (dc2 t d) -> p t dc2 d", dc2=2, d=128),
            ps_T4[:, 0:nblk * 128].rearrange("p (t dc2 d) -> p t dc2 d", dc2=2, d=128)),
            waits=[(st, trm)], inc=sv)
        atok_tr = st.n
    AF_ALL = sv.n
    scB2a.close()
    scB1p.close()

    scD = contextlib.ExitStack()
    ps_out = [scD.enter_context(nc.psum_tensor(f"ps_out{t}", [128, 512], F32))
              for t in range(OTS)]
    du_used = [0, 0]
    evac_prev = 0
    xo_cnt = 0
    for ob in range(OBLK):
        for ts in range(OTS):
            w = []
            if ts == 0:
                w = [(sv, evac_prev), (sv, HDN_ALL), (sv, AF_ALL), (sv, WO_FOLD),
                     (sSmall, SMALL_ALL), (sg, SG_CONST)]
            kb.emit("tensor", lambda e, ts=ts, ob=ob: e.matmul(
                ps_out[ts][:], ones1[:], borow_sb[0:1, ob * 512:(ob + 1) * 512],
                start=True, stop=True),
                waits=w)
        for c in range(DCH):
            for ts in range(OTS):
                kb.emit("tensor", lambda e, c=c, ts=ts, ob=ob: e.matmul(
                    ps_out[ts][:],
                    sbAF[:, c * OWN + ts * 128: c * OWN + ts * 128 + 128],
                    sbWOT[:, c * 2048 + ob * 512: c * 2048 + ob * 512 + 512],
                    start=False, stop=True, skip_group_check=True))
        for ich in range(ICH):
            b = ich % 2
            kb.emit("sync", lambda e, b=b, ich=ich, ob=ob: e.dma_start(
                out=sbDU[b][:], in_=down_bf[ob * 512:(ob + 1) * 512,
                                            ich * 128:(ich + 1) * 128],
                transpose=True),
                waits=[(st, du_used[b]), (sScr, SCR_ALL)] +
                      ([(sv, HDN_ALL)] if (ob == 0 and ich < 2) else []),
                inc=sDUp[b], n=16)
            for ts in range(OTS):
                kb.emit("tensor", lambda e, b=b, ich=ich, ts=ts: e.matmul(
                    ps_out[ts][:],
                    hdn[:, ich * OWN + ts * 128: ich * OWN + ts * 128 + 128],
                    sbDU[b][:],
                    start=False, stop=True, skip_group_check=True),
                    waits=([(sDUp[b], sDUp[b].n)] if ts == 0 else []),
                    inc=(st if ts == OTS - 1 else None))
            du_used[b] = st.n
        down_all = st.n
        for ts in range(OTS):
            b = ts % 2
            kb.emit("sync", lambda e, b=b, ts=ts, ob=ob: e.dma_start(
                out=sbXO[b][:], in_=x_own[ts * 128:(ts + 1) * 128,
                                          ob * 512:(ob + 1) * 512]),
                waits=[(sv, evac_prev if ts < 2 else sv.n)], inc=sXOp[b], n=16)
            kb.emit("vector", lambda e, b=b, ts=ts: e.tensor_add(
                sbOut[b][:], ps_out[ts][:], sbXO[b][:]),
                waits=[(st, down_all), (sXOp[b], sXOp[b].n),
                       (sOutp[b], sOutp[b].n)], inc=sv)
            kb.emit("sync", lambda e, b=b, ts=ts, ob=ob: e.dma_start(
                out=out_t[ts * 128:(ts + 1) * 128, ob * 512:(ob + 1) * 512],
                in_=sbOut[b][:]),
                waits=[(sv, sv.n)], inc=sOutp[b], n=16)
        evac_prev = sv.n
    scD.close()
    scB2.close()
    scB.close()

    kb.emit("sync", None, waits=[(sOutp[0], sOutp[0].n), (sOutp[1], sOutp[1].n)])

    with nc.Block() as block:
        @block.sync
        def _(e):
            for f in kb.ops["sync"]:
                f(e)

        @block.gpsimd
        def _(e):
            for f in kb.ops["gpsimd"]:
                f(e)

        @block.tensor
        def _(e):
            for f in kb.ops["tensor"]:
                f(e)

        @block.scalar
        def _(e):
            for f in kb.ops["scalar"]:
                f(e)

        @block.vector
        def _(e):
            for f in kb.ops["vector"]:
                f(e)

    return nc


def _prep_inputs(inp, T):
    OWN = T // NCORES
    ct = np.ascontiguousarray
    f32 = np.float32
    x = ct(np.asarray(inp["L0_attn_x"], dtype=f32).reshape(T, D))
    Wq = np.asarray(inp["L0_attn_Wq"], f32)
    Wk = np.asarray(inp["L0_attn_Wk"], f32)
    Wv = np.asarray(inp["L0_attn_Wv"], f32)
    Wo = np.asarray(inp["L0_attn_Wo"], f32)
    woT = ct(Wo.T)
    gateT = ct(np.asarray(inp["L0_ffn_gate_w"], f32).T)
    upT = ct(np.asarray(inp["L0_ffn_up_w"], f32).T)
    down_w = ct(np.asarray(inp["L0_ffn_down_w"], f32))
    gates = ct(np.asarray(inp["L0_ffn_gate_s"], f32).reshape(ICH, 128).T)
    ups = ct(np.asarray(inp["L0_ffn_up_s"], f32).reshape(ICH, 128).T)
    downs = ct(np.asarray(inp["L0_ffn_down_s"], f32).reshape(DCH, 128).T)
    borow = ct(np.asarray(inp["L0_attn_bo"], f32).reshape(1, D))
    aoT = ct(np.asarray(inp["L0_attn_Ao"], f32).T)
    bom = ct(np.asarray(inp["L0_attn_Bo"], f32))

    in_maps = []
    for c in range(NCORES):
        sl = slice(256 * c, 256 * (c + 1))
        bq = np.asarray(inp["L0_attn_bq"], f32)[sl]
        bk = np.asarray(inp["L0_attn_bk"], f32)[sl]
        bv = np.asarray(inp["L0_attn_bv"], f32)[sl]
        bqkv = ct(np.stack([bq[0:128], bq[128:256], bk[0:128], bk[128:256],
                            bv[0:128], bv[128:256]], axis=1))
        m = {
            "x": x,
            "x_own": ct(x[OWN * c: OWN * (c + 1)]),
            "wqT": ct(Wq[sl].T), "wkT": ct(Wk[sl].T), "wvT": ct(Wv[sl].T),
            "woT": woT,
            "bqkv": bqkv,
            "bvrow": ct(bv.reshape(1, 256)),
            "borow": borow,
            "aqT": ct(np.asarray(inp["L0_attn_Aq"], f32)[sl].T),
            "akT": ct(np.asarray(inp["L0_attn_Ak"], f32)[sl].T),
            "avT": ct(np.asarray(inp["L0_attn_Av"], f32)[sl].T),
            "aoT": aoT,
            "bqm": np.asarray(inp["L0_attn_Bq"], f32),
            "bkm": np.asarray(inp["L0_attn_Bk"], f32),
            "bvm": np.asarray(inp["L0_attn_Bv"], f32),
            "bom": bom,
            "gateT": gateT, "upT": upT, "down_w": down_w,
            "gates": gates, "ups": ups, "downs": downs,
            "coff": np.array([[OWN * c]], dtype=np.uint32),
        }
        in_maps.append(m)
    return in_maps


def kernel(**inputs):
    T = int(np.asarray(inputs["L0_attn_x"]).shape[1])
    nc = _build(T)
    in_maps = _prep_inputs(inputs, T)
    res = run_bass_kernel_spmd(nc, in_maps, core_ids=list(range(NCORES)))
    out = np.concatenate([res.results[c]["out"] for c in range(NCORES)], axis=0)
    return out.reshape(1, T, D).astype(np.float32)


# version 03 (final)
# speedup vs baseline: 1.3539x; 1.3539x over previous
import contextlib
import numpy as np
import concourse.bass as bass
import concourse.mybir as mybir
from concourse.bass_utils import run_bass_kernel_spmd

F32 = mybir.dt.float32
BF16 = mybir.dt.bfloat16
U32 = mybir.dt.uint32
AF = mybir.ActivationFunctionType
ALU = mybir.AluOpType

D = 2048
HD = 128
NCORES = 8
INTER = 5632
RANK = 16
SCALE = float(HD) ** -0.5
EPS = 1e-5
DCH = D // 128
ICH = INTER // 128
ISTR = INTER // 512

ENGS = ("sync", "gpsimd", "tensor", "scalar", "vector")


class Sem:
    __slots__ = ("h", "n")

    def __init__(self, h):
        self.h = h
        self.n = 0


class KB:
    def __init__(self, nc):
        self.nc = nc
        self.ops = {e: [] for e in ENGS}
        self.sems = {}

    def sem(self, name):
        s = self.sems.get(name)
        if s is None:
            s = self.sems[name] = Sem(self.nc.alloc_semaphore(name))
        return s

    def emit(self, eng, fn, waits=(), inc=None, n=1):
        wl = [(s.h, v) for (s, v) in waits if v > 0]
        ih, inn = (inc.h, n) if inc is not None else (None, 0)
        if inc is not None:
            inc.n += n

        def run(e, fn=fn, wl=wl, ih=ih, inn=inn):
            for h, v in wl:
                e.wait_ge(h, v)
            if fn is not None:
                ins = fn(e)
                if ih is not None:
                    ins.then_inc(ih, inn)
        self.ops[eng].append(run)


def _build(T):
    OWN = T // NCORES
    NKT = T // 128
    NSTRIP = T // 512
    OTS = OWN // 128
    NSEG = T // 512
    NGRP = NKT // 2
    OBLK = 4
    assert OWN % 512 == 0

    nc = bass.Bass("TRN2", target_bir_lowering=False)

    x_in = nc.dram_tensor("x", [T, D], F32, kind="ExternalInput")
    x_own = nc.dram_tensor("x_own", [OWN, D], F32, kind="ExternalInput")
    wT = {p: nc.dram_tensor(f"w{p}T", [D, 256], F32, kind="ExternalInput")
          for p in ("q", "k", "v")}
    woT = nc.dram_tensor("woT", [D, D], F32, kind="ExternalInput")
    bqkv = nc.dram_tensor("bqkv", [128, 6], F32, kind="ExternalInput")
    bvrow = nc.dram_tensor("bvrow", [1, 256], F32, kind="ExternalInput")
    borow = nc.dram_tensor("borow", [1, D], F32, kind="ExternalInput")
    aT = {p: nc.dram_tensor(f"a{p}T", [RANK, 256], F32, kind="ExternalInput")
          for p in ("q", "k", "v")}
    aoT_d = nc.dram_tensor("aoT", [RANK, D], F32, kind="ExternalInput")
    bm = {p: nc.dram_tensor(f"b{p}m", [RANK, D], F32, kind="ExternalInput")
          for p in ("q", "k", "v")}
    bom_d = nc.dram_tensor("bom", [RANK, D], F32, kind="ExternalInput")
    gateT = nc.dram_tensor("gateT", [D, INTER], F32, kind="ExternalInput")
    upT = nc.dram_tensor("upT", [D, INTER], F32, kind="ExternalInput")
    down_in = nc.dram_tensor("down_w", [D, INTER], F32, kind="ExternalInput")
    gates_in = nc.dram_tensor("gates", [128, ICH], F32, kind="ExternalInput")
    ups_in = nc.dram_tensor("ups", [128, ICH], F32, kind="ExternalInput")
    downs_in = nc.dram_tensor("downs", [128, DCH], F32, kind="ExternalInput")
    coff = nc.dram_tensor("coff", [1, 1], U32, kind="ExternalInput")

    out_t = nc.dram_tensor("out", [OWN, D], F32, kind="ExternalOutput")

    down_bf = nc.dram_tensor("down_bf", [D, INTER], BF16)
    cc_in = nc.dram_tensor("cc_in", [T, 256], BF16)
    cc_out = nc.dram_tensor("cc_out", [NCORES * T, 256], BF16)

    kb = KB(nc)
    S = kb.sem
    A = nc.alloc_sbuf_tensor

    ones1 = A("ones1", [1, 128], BF16)
    onescol = A("onescol", [128, 1], BF16)
    ident = A("ident", [128, 128], BF16)
    borow_sb = A("borow_sb", [1, D], BF16)
    gates_sb = A("gates_sb", [128, ICH], F32)
    ups_sb = A("ups_sb", [128, ICH], F32)
    downs_sb = A("downs_sb", [128, DCH], F32)
    coff_sb = A("coff_sb", [1, 1], U32)
    negM = A("negM", [128, 2], F32)
    sbEps = A("sbEps", [128, 1], F32)
    sbZero = A("sbZero", [128, 1], F32)

    sv, ss, st, sg, scc = S("sv"), S("ss"), S("st"), S("sg"), S("scc")
    sSmall, sCastW, sCastB, sCastD, sScr = (S("sSmall"), S("sCastW"), S("sCastB"), S("sCastD"), S("sScr"))
    sXa, sXb = S("sXa"), S("sXb")
    sXsem = [sXa, sXb]
    sCCs, sDyn, sWoA = S("sCCs"), S("sDyn"), S("sWoA")
    sWoW = [S("sWoWa"), S("sWoWb")]
    sWoB = [S("sWoBa"), S("sWoBb")]
    sDUp = [S("sDUa"), S("sDUb")]
    sXOp = [S("sXOa"), S("sXOb")]
    sOutp = [S("sOuta"), S("sOutb")]
    sGUs = [S("sGUa"), S("sGUb")]

    kb.emit("gpsimd", lambda e: e.memset(ones1[:], 1.0), inc=sg)
    kb.emit("gpsimd", lambda e: e.memset(onescol[:], 1.0), inc=sg)
    kb.emit("gpsimd", lambda e: e.memset(ident[:], 0.0), inc=sg)
    kb.emit("gpsimd", lambda e: e.memset(sbEps[:], EPS), inc=sg)
    kb.emit("gpsimd", lambda e: e.memset(sbZero[:], 0.0), inc=sg)
    kb.emit("gpsimd", lambda e: e.affine_select(
        out=ident[:], in_=ident[:], compare_op=ALU.not_equal, fill=1.0,
        base=0, pattern=[[-1, 128]], channel_multiplier=1),
        waits=[(sg, 3)], inc=sg)
    SG_CONST = sg.n

    for (dst, src) in ((gates_sb, gates_in), (ups_sb, ups_in),
                       (downs_sb, downs_in), (coff_sb, coff)):
        kb.emit("gpsimd", lambda e, d=dst, s=src: e.dma_start(out=d[:], in_=s[:]),
                inc=sSmall, n=16)
    kb.emit("gpsimd", lambda e: e.dma_start(out=borow_sb[:], in_=borow[:],
            max_dma_last_dim=4096), inc=sSmall, n=16)

    scA = contextlib.ExitStack()
    AA = lambda name, shape, dt: scA.enter_context(nc.sbuf_tensor(name, shape, dt))

    wt = {p: AA(f"wt_{p}", [128, DCH * 256], BF16) for p in ("q", "k", "v")}
    xnf = AA("xnf", [128, DCH * 512], BF16)
    QT = [AA(f"QT{h}", [128, T], BF16) for h in range(2)]
    KT = [AA(f"KT{h}", [128, T], BF16) for h in range(2)]
    VA = [AA(f"VA{h}", [128, NKT * 129], BF16) for h in range(2)]
    sbP = [AA(f"sbP{i}", [128, 1024], BF16) for i in range(2)]
    bqkv_sb = AA("bqkv_sb", [128, 6], F32)
    bvrow_sb = AA("bvrow_sb", [1, 256], BF16)
    o_sb = [AA(f"o_sb{i}", [128, 512], BF16) for i in range(2)]
    sq_sb = [AA(f"sq_sb{i}", [128, 512], BF16) for i in range(2)]
    sbRec = AA("sbRec", [128, 8], F32)
    sbMseg = AA("sbMseg", [1, 1], F32)
    sbRunQ = AA("sbRunQ", [1, 1], F32)
    sbRunK = AA("sbRunK", [1, 1], F32)
    sbProd = AA("sbProd", [1, 1], F32)
    sbMx = AA("sbMx", [1, 1], F32)
    sbMbf = AA("sbMbf", [1, 1], BF16)
    lnA = {
        "x": [AA(f"lnx{i}", [128, D], F32) for i in range(2)],
        "xn": [AA(f"lnxn{i}", [128, D], BF16) for i in range(2)],
        "bn6": AA("lnbn6", [128, 24], F32),
        "mv": AA("lnmv", [128, 2], F32),
        "std": AA("lnstd", [128, 1], F32),
        "rinv": AA("lnrinv", [128, 1], F32),
        "nmr": AA("lnnmr", [128, 1], F32),
    }

    kb.emit("gpsimd", lambda e: e.dma_start(out=bqkv_sb[:], in_=bqkv[:]),
            inc=sSmall, n=16)
    kb.emit("gpsimd", lambda e: e.dma_start(out=bvrow_sb[:], in_=bvrow[:],
            max_dma_last_dim=4096), inc=sSmall, n=16)
    SMALL_ALL = sSmall.n
    SMALL_A = sSmall.n

    for h in range(2):
        kb.emit("gpsimd", lambda e, h=h: e.memset(
            VA[h][:].rearrange("p (k d) -> p k d", d=129)[:, :, 128:129], 1.0), inc=sg)
    VA_ONES = sg.n

    wnat2 = [AA(f"wnat2_{i}", [128, 2816], BF16) for i in range(2)]

    scA0 = contextlib.ExitStack()
    A0 = lambda name, shape, dt: scA0.enter_context(nc.sbuf_tensor(name, shape, dt))
    sbATl = {p: A0(f"sbAT_{p}", [RANK, 256], BF16) for p in ("q", "k", "v")}
    sbBm = A0("sbBm", [RANK, D], BF16)
    psm1 = contextlib.ExitStack()
    ps_T = psm1.enter_context(nc.psum_tensor("ps_T", [128, 2048], BF16))
    ps_qk = [psm1.enter_context(nc.psum_tensor(f"ps_qk{i}", [128, 512], F32))
             for i in range(2)]
    ps_v = psm1.enter_context(nc.psum_tensor("ps_v", [128, 256], F32))
    ps_ab = psm1.enter_context(nc.psum_tensor("ps_ab", [128, 256], F32))

    for p in ("q", "k", "v"):
        kb.emit("gpsimd", lambda e, d=wt[p], s=wT[p]: e.dma_start(
            out=d[:].rearrange("p (c o) -> p c o", o=256),
            in_=s[:].rearrange("(c p) o -> p c o", p=128),
            max_dma_last_dim=4096), inc=sCastW, n=16)
        kb.emit("gpsimd", lambda e, d=sbATl[p], s=aT[p]: e.dma_start(
            out=d[:], in_=s[:], max_dma_last_dim=4096), inc=sCastW, n=16)
    CAST_W = sCastW.n

    fold_v_prev = 0
    for ip, p in enumerate(("q", "k", "v")):
        kb.emit("gpsimd", lambda e, s=bm[p]: e.dma_start(
            out=sbBm[:], in_=s[:], max_dma_last_dim=4096),
            waits=[(sv, fold_v_prev)], inc=sCastB, n=16)
        bm_ready = sCastB.n
        for c in range(DCH):
            kb.emit("tensor", lambda e, p=p, c=c: e.matmul(
                ps_ab[:], sbBm[:, c * 128:(c + 1) * 128], sbATl[p][:],
                start=True, stop=True),
                waits=[(sCastW, CAST_W), (sCastB, bm_ready), (sv, sv.n)],
                inc=st)
            kb.emit("vector", lambda e, p=p, c=c: e.scalar_tensor_tensor(
                wt[p][:, c * 256:(c + 1) * 256], ps_ab[:], 0.5,
                wt[p][:, c * 256:(c + 1) * 256], op0=ALU.mult, op1=ALU.add),
                waits=[(st, st.n)], inc=sv)
        fold_v_prev = sv.n
    WT_READY = sv.n

    def emit_scr_round(rr):
        r, half = rr // 2, rr % 2
        buf = wnat2[rr % 2]
        c0 = half * 2816
        kb.emit("gpsimd", lambda e, b=buf, r=r, c0=c0: e.dma_start(
            out=b[:], in_=down_in[r * 128:(r + 1) * 128, c0:c0 + 2816],
            max_dma_last_dim=4096),
            waits=[(sScr, sScr.n), (sSmall, SMALL_ALL)], inc=sCastD, n=16)
        kb.emit("vector", lambda e, b=buf, r=r: e.tensor_scalar_mul(
            b[:], b[:], downs_sb[:, r:r + 1]),
            waits=[(sCastD, sCastD.n)], inc=sv)
        kb.emit("gpsimd", lambda e, b=buf, r=r, c0=c0: e.dma_start(
            out=down_bf[r * 128:(r + 1) * 128, c0:c0 + 2816], in_=b[:]),
            waits=[(sv, sv.n)], inc=sScr, n=16)


    ln_state = {"apply": [0, 0], "tr": [0, 0], "evac": 0}

    def ln_tile(src_fn, ti, ln, extra_x_waits=()):
        b = ti % 2
        kb.emit("sync", lambda e, s=src_fn: e.dma_start(out=ln["x"][b][:], in_=s()),
                waits=[(sv, ln_state["apply"][b])] + list(extra_x_waits),
                inc=sXsem[b], n=16)
        xw = (sXsem[b], sXsem[b].n)
        for k in range(4):
            kb.emit("vector", lambda e, k=k, b=b, ln=ln: e.bn_stats(
                ln["bn6"][:, 6 * k:6 * k + 6], ln["x"][b][:, 512 * k:512 * (k + 1)]),
                waits=([xw, (sv, sv.n), (ss, ss.n)] if k == 0 else []),
                inc=(sv if k == 3 else None))
        kb.emit("vector", lambda e, ln=ln: e.bn_aggr(ln["mv"][:], ln["bn6"][:]),
                waits=[(sv, sv.n)], inc=sv)
        kb.emit("scalar", lambda e, ln=ln: e.activation(
            ln["std"][:], ln["mv"][:, 1:2], AF.Sqrt, bias=sbEps[:]),
            waits=[(sv, sv.n), (sg, SG_CONST)], inc=ss)
        kb.emit("vector", lambda e, ln=ln: e.reciprocal(ln["rinv"][:], ln["std"][:]),
                waits=[(ss, ss.n)], inc=sv)
        kb.emit("vector", lambda e, ln=ln: e.scalar_tensor_tensor(
            ln["nmr"][:], ln["mv"][:, 0:1], -1.0, ln["rinv"][:],
            op0=ALU.mult, op1=ALU.mult),
            waits=[(sv, sv.n)], inc=sv)
        kb.emit("vector", lambda e, b=b, ln=ln: e.tensor_scalar(
            ln["xn"][b][:], ln["x"][b][:], ln["rinv"][:], ln["nmr"][:],
            op0=ALU.mult, op1=ALU.add),
            waits=[(sv, sv.n), (st, ln_state["tr"][b])], inc=sv)
        ln_state["apply"][b] = sv.n
        return sv.n

    def xn_transpose(ln, b, xn_ready, dst_ap_fn, war_st, psT):
        for c in range(DCH):
            kb.emit("tensor", lambda e, c=c, b=b, ln=ln, psT=psT: e.transpose(
                psT[:, 128 * c:128 * (c + 1)], ln["xn"][b][:, 128 * c:128 * (c + 1)],
                ident[:]),
                waits=([(sv, xn_ready), (sg, SG_CONST), (sv, ln_state["evac"])]
                       if c == 0 else []),
                inc=(st if c == DCH - 1 else None))
        trm = st.n
        ln_state["tr"][b] = trm
        kb.emit("vector", lambda e, psT=psT: e.tensor_copy(
            dst_ap_fn(), psT[:].rearrange("p (c d) -> p c d", d=128)),
            waits=[(st, trm), (st, war_st)], inc=sv)
        ln_state["evac"] = sv.n
        return sv.n

    qkv_after_strip = 0
    v_evac_last = 0
    for s_ in range(NSTRIP):
        for tl in range(4):
            ti = s_ * 4 + tl
            xn_ready = ln_tile(
                lambda ti=ti: x_in[ti * 128:(ti + 1) * 128, :], ti, lnA)
            xn_transpose(
                lnA, ti % 2, xn_ready,
                lambda tl=tl: xnf[:].rearrange(
                    "p (c t d) -> p c t d", t=4, d=128)[:, :, tl, :],
                qkv_after_strip, ps_T)
        strip_xnf = sv.n

        for p in ("q", "k"):
            for a in range(2):
                pq = ps_qk[a]
                for c in range(DCH):
                    w = ([(sv, strip_xnf), (sv, WT_READY), (ss, ss.n)]
                         if c == 0 else [])
                    kb.emit("tensor", lambda e, p=p, a=a, c=c, pq=pq: e.matmul(
                        pq[:], wt[p][:, c * 256 + a * 128: c * 256 + a * 128 + 128],
                        xnf[:, c * 512:(c + 1) * 512],
                        start=(c == 0), stop=True, skip_group_check=True),
                        waits=w, inc=(st if c == DCH - 1 else None))
                dst = QT[a] if p == "q" else KT[a]
                bc = (0 if p == "q" else 2) + a
                kb.emit("scalar", lambda e, d=dst, s_=s_, pq=pq, bc=bc: e.activation(
                    d[:, s_ * 512:(s_ + 1) * 512], pq[:], AF.Identity,
                    bias=bqkv_sb[:, bc:bc + 1], scale=1.0),
                    waits=[(st, st.n), (sSmall, SMALL_A)], inc=ss)
        for tl in range(4):
            kt = s_ * 4 + tl
            kb.emit("tensor", lambda e: e.matmul(
                ps_v[:], ones1[:], bvrow_sb[:], start=True, stop=True),
                waits=[(sv, v_evac_last), (sg, SG_CONST), (sSmall, SMALL_A),
                       (sv, strip_xnf)])
            for c in range(DCH):
                kb.emit("tensor", lambda e, tl=tl, c=c: e.matmul(
                    ps_v[:], xnf[:, c * 512 + tl * 128: c * 512 + tl * 128 + 128],
                    wt["v"][:, c * 256:(c + 1) * 256],
                    start=False, stop=True, skip_group_check=True),
                    inc=(st if c == DCH - 1 else None))
            for h in range(2):
                kb.emit("vector", lambda e, h=h, kt=kt: e.tensor_copy(
                    VA[h][:, 129 * kt:129 * kt + 128], ps_v[:, h * 128:(h + 1) * 128]),
                    waits=([(st, st.n)] if h == 0 else []),
                    inc=(sv if h == 1 else None))
            v_evac_last = sv.n
        qkv_after_strip = st.n
    P1_V_ALL = sv.n
    P1_QK_ALL = ss.n
    scA0.close()
    psm1.close()

    scN = contextlib.ExitStack()
    ps_n = [scN.enter_context(nc.psum_tensor(f"ps_n{i}", [1, 512], F32))
            for i in range(2)]
    ps_nm = scN.enter_context(nc.psum_tensor("ps_nm", [128, 1], F32))

    for h in range(2):
        kb.emit("gpsimd", lambda e: e.memset(sbRunQ[:], 0.0),
                waits=[(sv, sv.n)], inc=sg)
        kb.emit("gpsimd", lambda e: e.memset(sbRunK[:], 0.0), inc=sg)
        run_init = sg.n
        for (mat, run) in ((QT[h], sbRunQ), (KT[h], sbRunK)):
            for seg in range(NSEG):
                b = seg % 2
                kb.emit("scalar", lambda e, m=mat, seg=seg, b=b: e.activation(
                    sq_sb[b][:], m[:, seg * 512:(seg + 1) * 512], AF.Square,
                    bias=sbZero[:]),
                    waits=[(ss, P1_QK_ALL), (st, st.n)], inc=ss)
                kb.emit("tensor", lambda e, b=b: e.matmul(
                    ps_n[b][:], onescol[:], sq_sb[b][:], start=True, stop=True),
                    waits=[(ss, ss.n), (sv, sv.n), (sg, SG_CONST)], inc=st)
                kb.emit("vector", lambda e, b=b: e.reduce_max(
                    sbMseg[:], ps_n[b][:], axis=mybir.AxisListType.X),
                    waits=[(st, st.n), (sg, run_init), (sv, sv.n)], inc=sv)
                kb.emit("vector", lambda e, r=run: e.tensor_max(r[:], r[:], sbMseg[:]),
                        waits=[(sv, sv.n)], inc=sv)
        kb.emit("vector", lambda e: e.tensor_mul(sbProd[:], sbRunQ[:], sbRunK[:]),
                waits=[(sv, sv.n)], inc=sv)
        kb.emit("scalar", lambda e: e.activation(
            sbMx[:], sbProd[:], AF.Sqrt, bias=sbZero[0:1, :]),
                waits=[(sv, sv.n)], inc=ss)
        kb.emit("vector", lambda e: e.tensor_scalar_mul(sbMbf[:], sbMx[:], -SCALE),
                waits=[(ss, ss.n)], inc=sv)
        kb.emit("tensor", lambda e: e.matmul(
            ps_nm[:], ones1[:], sbMbf[:], start=True, stop=True),
            waits=[(sv, sv.n)], inc=st)
        kb.emit("vector", lambda e, h=h: e.tensor_copy(negM[:, h:h + 1], ps_nm[:]),
                waits=[(st, st.n)], inc=sv)
    NEGM_READY = sv.n
    scN.close()

    scAT = contextlib.ExitStack()
    ps_s = [scAT.enter_context(nc.psum_tensor(f"ps_s{i}", [128, 1024], F32))
            for i in range(2)]
    ps_oa = [scAT.enter_context(nc.psum_tensor(f"ps_oa{i}", [128, 512], F32))
             for i in range(2)]
    ps_ob = [scAT.enter_context(nc.psum_tensor(f"ps_ob{i}", [128, 512], F32))
             for i in range(2)]

    groups = [(h, qs, g) for h in range(2) for qs in range(NSTRIP)
              for g in range(NGRP)]
    prev_exp = [0, 0]
    prev_av = [0, 0]
    prev_norm = [0, 0]
    cc_ms = [0, 0]
    s_ms = [0, 0]
    scr_next = 0

    def emit_sgrp(idx):
        h, qs, g = groups[idx]
        par = g % 2
        for kl in range(2):
            kt = 2 * g + kl
            w = []
            if kl == 0:
                w = [(ss, prev_exp[par])]
                if idx == 0:
                    w += [(ss, P1_QK_ALL), (sv, NEGM_READY)]
            kb.emit("tensor", lambda e, h=h, qs=qs, kt=kt, par=par, kl=kl:
                    e.matmul(
                ps_s[par][:, kl * 512:(kl + 1) * 512],
                KT[h][:, kt * 128:(kt + 1) * 128],
                QT[h][:, qs * 512:(qs + 1) * 512],
                start=True, stop=True, skip_group_check=True),
                waits=w, inc=(st if kl == 1 else None))
        s_ms[par] = st.n

    emit_sgrp(0)
    for idx, (h, qs, g) in enumerate(groups):
        par = g % 2
        spar = (h * NSTRIP + qs) % 2
        wex = [(st, s_ms[par]), (st, prev_av[par])]
        if idx == 0:
            wex.append((sv, NEGM_READY))
        kb.emit("scalar", lambda e, h=h, par=par: e.activation(
            sbP[par][:], ps_s[par][:], AF.Exp,
            bias=negM[:, h:h + 1], scale=SCALE),
            waits=wex, inc=ss)
        prev_exp[par] = ss.n
        if idx + 1 < len(groups):
            emit_sgrp(idx + 1)
        wav = [(ss, prev_exp[par])]
        if g == 0:
            wav += [(sv, P1_V_ALL), (sg, VA_ONES), (sv, prev_norm[spar])]
        first = True
        for kl in range(2):
            kt = 2 * g + kl
            for j in range(4):
                stt = (g == 0 and kl == 0 and j in (0, 3))
                kb.emit("tensor", lambda e, h=h, kt=kt, kl=kl, j=j, par=par,
                        spar=spar, stt=stt: e.matmul(
                    (ps_oa[spar][:, 129 * j:129 * j + 129] if j < 3
                     else ps_ob[spar][:, 0:129]),
                    sbP[par][:, kl * 512 + j * 128: kl * 512 + j * 128 + 128],
                    VA[h][:, 129 * kt:129 * kt + 129],
                    start=stt, stop=True, skip_group_check=True),
                    waits=(wav if first else []),
                    inc=(st if (kl == 1 and j == 3) else None))
                first = False
        prev_av[par] = st.n
        if g == NGRP - 1:
            av_all = st.n
            for j in range(4):
                kb.emit("vector", lambda e, j=j, spar=spar: e.reciprocal(
                    sbRec[:, spar * 4 + j:spar * 4 + j + 1],
                    (ps_oa[spar][:, 129 * j + 128:129 * j + 129] if j < 3
                     else ps_ob[spar][:, 128:129])),
                    waits=([(st, av_all)] if j == 0 else []),
                    inc=(sv if j == 3 else None))
            rcp = sv.n
            for j in range(4):
                kb.emit("vector", lambda e, j=j, spar=spar: e.tensor_scalar_mul(
                    o_sb[spar][:, j * 128:(j + 1) * 128],
                    (ps_oa[spar][:, 129 * j:129 * j + 128] if j < 3
                     else ps_ob[spar][:, 0:128]),
                    sbRec[:, spar * 4 + j:spar * 4 + j + 1]),
                    waits=([(sv, rcp), (sCCs, cc_ms[spar])] if j == 0 else []),
                    inc=(sv if j == 3 else None))
            prev_norm[spar] = sv.n
            kb.emit("sync", lambda e, h=h, qs=qs, spar=spar: e.dma_start(
                out=cc_in[qs * 512:(qs + 1) * 512, h * 128:(h + 1) * 128]
                    .rearrange("(a p) d -> p a d", p=128),
                in_=o_sb[spar][:].rearrange("p (a d) -> p a d", d=128)),
                waits=[(sv, prev_norm[spar])], inc=sCCs, n=16)
            cc_ms[spar] = sCCs.n
            if scr_next < 2 * DCH:
                emit_scr_round(scr_next)
                scr_next += 1
    CC_ALL = sCCs.n
    SCR_ALL = sScr.n
    scAT.close()
    scA.close()

    kb.emit("gpsimd", lambda e: e.collective_compute(
        "AllGather", ALU.bypass, replica_groups=[list(range(NCORES))],
        ins=[cc_in[:].opt()], outs=[cc_out[:].opt()]),
        waits=[(sCCs, CC_ALL)], inc=scc)

    scB = contextlib.ExitStack()
    hdn = scB.enter_context(nc.sbuf_tensor("hdn", [128, ICH * OWN], BF16))

    scB1 = contextlib.ExitStack()
    B1 = lambda name, shape, dt: scB1.enter_context(nc.sbuf_tensor(name, shape, dt))
    xnf_own = B1("xnf_own", [128, DCH * OWN], BF16)
    sbStrip = [B1(f"sbStrip{i}", [128, DCH * 512], BF16) for i in range(2)]
    sbSilu = B1("sbSilu", [128, 4 * OWN], BF16)
    lnB = {
        "x": [B1(f"lnBx{i}", [128, D], F32) for i in range(2)],
        "xn": [B1(f"lnBxn{i}", [128, D], BF16) for i in range(2)],
        "bn6": B1("lnBbn6", [128, 24], F32),
        "mv": B1("lnBmv", [128, 2], F32),
        "std": B1("lnBstd", [128, 1], F32),
        "rinv": B1("lnBrinv", [128, 1], F32),
        "nmr": B1("lnBnmr", [128, 1], F32),
    }
    scB1p = contextlib.ExitStack()
    ps_T4 = scB1p.enter_context(nc.psum_tensor("ps_T4", [128, 2048], BF16))
    ps_g = scB1p.enter_context(nc.psum_tensor("ps_g", [128, OWN], F32))
    ps_u = scB1p.enter_context(nc.psum_tensor("ps_u", [128, OWN], F32))
    assert OWN <= 1024

    ln_state["tr"] = [0, 0]
    for ti in range(OTS):
        xn_ready = ln_tile(
            lambda ti=ti: x_own[ti * 128:(ti + 1) * 128, :], ti, lnB,
            extra_x_waits=[(sCCs, CC_ALL), (sScr, SCR_ALL)])
        xn_transpose(
            lnB, ti % 2, xn_ready,
            lambda ti=ti: xnf_own[:].rearrange(
                "p (c t d) -> p c t d", t=OTS, d=128)[:, :, ti, :],
            0, ps_T4)
    XNF_OWN = sv.n

    silu_ms = 0
    hdn_ms = 0
    gu_after = [0, 0]
    NB = OWN // 512
    for istr in range(ISTR):
        for iw in range(2):
            wsrc = gateT if iw == 0 else upT
            psx = ps_g if iw == 0 else ps_u
            buf = sbStrip[iw]
            wgu = [(st, gu_after[iw])]
            if istr == 0:
                wgu += [(sScr, SCR_ALL), (sCCs, CC_ALL)]
            kb.emit("gpsimd", lambda e, b=buf, s=wsrc, istr=istr: e.dma_start(
                out=b[:].rearrange("p (c i) -> p c i", i=512),
                in_=s[:, istr * 512:(istr + 1) * 512].rearrange("(c p) i -> p c i", p=128),
                max_dma_last_dim=4096),
                waits=wgu, inc=sGUs[iw], n=16)
            gu_ready = sGUs[iw].n
            for it in range(4):
                ich = istr * 4 + it
                for c in range(DCH):
                    for nb in range(NB):
                        w = []
                        if c == 0 and nb == 0:
                            w = [(sGUs[iw], gu_ready), (sv, XNF_OWN),
                                 (sCCs, CC_ALL)]
                            w.append((ss, silu_ms) if iw == 0 else (sv, hdn_ms))
                        kb.emit("tensor", lambda e, b=buf, c=c, it=it, nb=nb, psx=psx:
                                e.matmul(
                            psx[:, nb * 512:(nb + 1) * 512],
                            b[:, c * 512 + it * 128: c * 512 + it * 128 + 128],
                            xnf_own[:, c * OWN + nb * 512: c * OWN + nb * 512 + 512],
                            start=(c == 0), stop=True, skip_group_check=True),
                            waits=w,
                            inc=(st if (c == DCH - 1 and nb == NB - 1) else None))
                mm_ms = st.n
                if iw == 0:
                    kb.emit("scalar", lambda e, ich=ich, it=it: e.activation(
                        sbSilu[:, it * OWN:(it + 1) * OWN], ps_g[:], AF.Silu,
                        bias=sbZero[:], scale=gates_sb[:, ich:ich + 1]),
                        waits=[(st, mm_ms), (sv, hdn_ms), (sSmall, SMALL_ALL)],
                        inc=ss)
                    silu_ms = ss.n
                else:
                    kb.emit("vector", lambda e, ich=ich, it=it: e.scalar_tensor_tensor(
                        hdn[:, ich * OWN:(ich + 1) * OWN], ps_u[:],
                        ups_sb[:, ich:ich + 1], sbSilu[:, it * OWN:(it + 1) * OWN],
                        op0=ALU.mult, op1=ALU.mult),
                        waits=[(st, mm_ms), (ss, silu_ms)], inc=sv)
                    hdn_ms = sv.n
            gu_after[iw] = st.n
    HDN_ALL = sv.n
    scB1.close()

    scB2 = contextlib.ExitStack()
    B2 = lambda name, shape, dt: scB2.enter_context(nc.sbuf_tensor(name, shape, dt))
    sbWOT = B2("sbWOT", [128, DCH * 2048], BF16)
    sbAF = B2("sbAF", [128, DCH * OWN], BF16)
    sbATok = B2("sbATok", [128, OTS * 256], BF16)
    sbAoT = B2("sbAoT", [RANK, D], BF16)
    sbBo2 = [B2(f"sbBo2_{i}", [RANK, 128], BF16) for i in range(2)]
    sbDU = [B2(f"sbDU{i}", [128, 512], BF16) for i in range(2)]
    sbXO = [B2(f"sbXO{i}", [128, 512], F32) for i in range(2)]
    sbOut = [B2(f"sbOut{i}", [128, 512], F32) for i in range(2)]
    scB2a = contextlib.ExitStack()
    ps_ab4 = scB2a.enter_context(nc.psum_tensor("ps_ab4", [128, 256], F32))

    kb.emit("gpsimd", lambda e: e.dma_start(out=sbAoT[:], in_=aoT_d[:],
            max_dma_last_dim=4096), waits=[(sv, HDN_ALL)], inc=sWoA, n=16)
    fold_st = [0] * DCH
    for c in range(DCH):
        par = c % 2
        kb.emit("gpsimd", lambda e, c=c: e.dma_start(
            out=sbWOT[:, c * 2048:(c + 1) * 2048],
            in_=woT[c * 128:(c + 1) * 128, :], max_dma_last_dim=4096),
            waits=([(st, fold_st[c - 2])] if c >= 2 else []),
            inc=sWoW[par], n=16)
        kb.emit("gpsimd", lambda e, c=c, par=par: e.dma_start(
            out=sbBo2[par][:], in_=bom_d[:, c * 128:(c + 1) * 128],
            max_dma_last_dim=4096),
            waits=([(st, fold_st[c - 1])] if c >= 1 else []),
            inc=sWoB[par], n=16)
        wcnt = sWoW[par].n
        bcnt = sWoB[par].n
        for sgm in range(8):
            w = [(sv, sv.n)]
            if sgm == 0:
                w += [(sWoW[par], wcnt), (sWoB[par], bcnt), (sWoA, 16),
                      (sCCs, CC_ALL)]
            kb.emit("tensor", lambda e, c=c, sgm=sgm, par=par: e.matmul(
                ps_ab4[:], sbBo2[par][:], sbAoT[:, sgm * 256:(sgm + 1) * 256],
                start=True, stop=True),
                waits=w, inc=st)
            kb.emit("vector", lambda e, c=c, sgm=sgm: e.scalar_tensor_tensor(
                sbWOT[:, c * 2048 + sgm * 256: c * 2048 + (sgm + 1) * 256],
                ps_ab4[:], 0.5,
                sbWOT[:, c * 2048 + sgm * 256: c * 2048 + (sgm + 1) * 256],
                op0=ALU.mult, op1=ALU.add),
                waits=[(st, st.n)], inc=sv)
        fold_st[c] = st.n
    WO_FOLD = sv.n

    atok_tr = 0
    for r in range(NCORES):
        kb.emit("gpsimd", lambda e, r=r: (lambda offv: e.dma_start(
            out=sbATok[:].rearrange("p (a d) -> p a d", d=256),
            in_=cc_out[bass.ds(offv + r * T, OWN), :]
                .rearrange("(a p) d -> p a d", p=128)))(
            e.value_load(coff_sb[0:1, 0:1])),
            waits=[(scc, 1), (sSmall, SMALL_ALL), (st, atok_tr)] +
                  ([(sv, HDN_ALL)] if r == 0 else []), inc=sDyn, n=16)
        nblk = OTS * 2
        for ts in range(OTS):
            for dc2 in range(2):
                bi = ts * 2 + dc2
                kb.emit("tensor", lambda e, ts=ts, dc2=dc2, bi=bi: e.transpose(
                    ps_T4[:, 128 * bi:128 * (bi + 1)],
                    sbATok[:, ts * 256 + dc2 * 128: ts * 256 + dc2 * 128 + 128],
                    ident[:]),
                    waits=([(sDyn, sDyn.n), (sv, sv.n)]
                           if (ts == 0 and dc2 == 0) else []),
                    inc=(st if (ts * 2 + dc2 + 1) == nblk else None))
        trm = st.n
        kb.emit("vector", lambda e, r=r: e.tensor_copy(
            sbAF[:, r * 2 * OWN:(r + 1) * 2 * OWN]
                .rearrange("p (dc2 t d) -> p t dc2 d", dc2=2, d=128),
            ps_T4[:, 0:nblk * 128].rearrange("p (t dc2 d) -> p t dc2 d", dc2=2, d=128)),
            waits=[(st, trm)], inc=sv)
        atok_tr = st.n
    AF_ALL = sv.n
    scB2a.close()
    scB1p.close()

    scD = contextlib.ExitStack()
    ps_out = [scD.enter_context(nc.psum_tensor(f"ps_out{t}", [128, 512], F32))
              for t in range(OTS)]
    du_used = [0, 0]
    evac_prev = 0
    xo_cnt = 0
    for ob in range(OBLK):
        for ts in range(OTS):
            w = []
            if ts == 0:
                w = [(sv, evac_prev), (sv, HDN_ALL), (sv, AF_ALL), (sv, WO_FOLD),
                     (sSmall, SMALL_ALL), (sg, SG_CONST)]
            kb.emit("tensor", lambda e, ts=ts, ob=ob: e.matmul(
                ps_out[ts][:], ones1[:], borow_sb[0:1, ob * 512:(ob + 1) * 512],
                start=True, stop=True),
                waits=w)
        for c in range(DCH):
            for ts in range(OTS):
                kb.emit("tensor", lambda e, c=c, ts=ts, ob=ob: e.matmul(
                    ps_out[ts][:],
                    sbAF[:, c * OWN + ts * 128: c * OWN + ts * 128 + 128],
                    sbWOT[:, c * 2048 + ob * 512: c * 2048 + ob * 512 + 512],
                    start=False, stop=True, skip_group_check=True))
        for ich in range(ICH):
            b = ich % 2
            kb.emit("sync", lambda e, b=b, ich=ich, ob=ob: e.dma_start(
                out=sbDU[b][:], in_=down_bf[ob * 512:(ob + 1) * 512,
                                            ich * 128:(ich + 1) * 128],
                transpose=True),
                waits=[(st, du_used[b]), (sScr, SCR_ALL)] +
                      ([(sv, HDN_ALL)] if (ob == 0 and ich < 2) else []),
                inc=sDUp[b], n=16)
            for ts in range(OTS):
                kb.emit("tensor", lambda e, b=b, ich=ich, ts=ts: e.matmul(
                    ps_out[ts][:],
                    hdn[:, ich * OWN + ts * 128: ich * OWN + ts * 128 + 128],
                    sbDU[b][:],
                    start=False, stop=True, skip_group_check=True),
                    waits=([(sDUp[b], sDUp[b].n)] if ts == 0 else []),
                    inc=(st if ts == OTS - 1 else None))
            du_used[b] = st.n
        down_all = st.n
        for ts in range(OTS):
            b = ts % 2
            kb.emit("sync", lambda e, b=b, ts=ts, ob=ob: e.dma_start(
                out=sbXO[b][:], in_=x_own[ts * 128:(ts + 1) * 128,
                                          ob * 512:(ob + 1) * 512]),
                waits=[(sv, evac_prev if ts < 2 else sv.n)], inc=sXOp[b], n=16)
            kb.emit("vector", lambda e, b=b, ts=ts: e.tensor_add(
                sbOut[b][:], ps_out[ts][:], sbXO[b][:]),
                waits=[(st, down_all), (sXOp[b], sXOp[b].n),
                       (sOutp[b], sOutp[b].n)], inc=sv)
            kb.emit("sync", lambda e, b=b, ts=ts, ob=ob: e.dma_start(
                out=out_t[ts * 128:(ts + 1) * 128, ob * 512:(ob + 1) * 512],
                in_=sbOut[b][:]),
                waits=[(sv, sv.n)], inc=sOutp[b], n=16)
        evac_prev = sv.n
    scD.close()
    scB2.close()
    scB.close()

    kb.emit("sync", None, waits=[(sOutp[0], sOutp[0].n), (sOutp[1], sOutp[1].n)])

    with nc.Block() as block:
        @block.sync
        def _(e):
            for f in kb.ops["sync"]:
                f(e)

        @block.gpsimd
        def _(e):
            for f in kb.ops["gpsimd"]:
                f(e)

        @block.tensor
        def _(e):
            for f in kb.ops["tensor"]:
                f(e)

        @block.scalar
        def _(e):
            for f in kb.ops["scalar"]:
                f(e)

        @block.vector
        def _(e):
            for f in kb.ops["vector"]:
                f(e)

    return nc


def _prep_inputs(inp, T):
    OWN = T // NCORES
    ct = np.ascontiguousarray
    f32 = np.float32
    x = ct(np.asarray(inp["L0_attn_x"], dtype=f32).reshape(T, D))
    Wq = np.asarray(inp["L0_attn_Wq"], f32)
    Wk = np.asarray(inp["L0_attn_Wk"], f32)
    Wv = np.asarray(inp["L0_attn_Wv"], f32)
    Wo = np.asarray(inp["L0_attn_Wo"], f32)
    woT = ct(Wo.T)
    gateT = ct(np.asarray(inp["L0_ffn_gate_w"], f32).T)
    upT = ct(np.asarray(inp["L0_ffn_up_w"], f32).T)
    down_w = ct(np.asarray(inp["L0_ffn_down_w"], f32))
    gates = ct(np.asarray(inp["L0_ffn_gate_s"], f32).reshape(ICH, 128).T)
    ups = ct(np.asarray(inp["L0_ffn_up_s"], f32).reshape(ICH, 128).T)
    downs = ct(np.asarray(inp["L0_ffn_down_s"], f32).reshape(DCH, 128).T)
    borow = ct(np.asarray(inp["L0_attn_bo"], f32).reshape(1, D))
    aoT = ct(np.asarray(inp["L0_attn_Ao"], f32).T)
    bom = ct(np.asarray(inp["L0_attn_Bo"], f32))

    in_maps = []
    for c in range(NCORES):
        sl = slice(256 * c, 256 * (c + 1))
        bq = np.asarray(inp["L0_attn_bq"], f32)[sl]
        bk = np.asarray(inp["L0_attn_bk"], f32)[sl]
        bv = np.asarray(inp["L0_attn_bv"], f32)[sl]
        bqkv = ct(np.stack([bq[0:128], bq[128:256], bk[0:128], bk[128:256],
                            bv[0:128], bv[128:256]], axis=1))
        m = {
            "x": x,
            "x_own": ct(x[OWN * c: OWN * (c + 1)]),
            "wqT": ct(Wq[sl].T), "wkT": ct(Wk[sl].T), "wvT": ct(Wv[sl].T),
            "woT": woT,
            "bqkv": bqkv,
            "bvrow": ct(bv.reshape(1, 256)),
            "borow": borow,
            "aqT": ct(np.asarray(inp["L0_attn_Aq"], f32)[sl].T),
            "akT": ct(np.asarray(inp["L0_attn_Ak"], f32)[sl].T),
            "avT": ct(np.asarray(inp["L0_attn_Av"], f32)[sl].T),
            "aoT": aoT,
            "bqm": np.asarray(inp["L0_attn_Bq"], f32),
            "bkm": np.asarray(inp["L0_attn_Bk"], f32),
            "bvm": np.asarray(inp["L0_attn_Bv"], f32),
            "bom": bom,
            "gateT": gateT, "upT": upT, "down_w": down_w,
            "gates": gates, "ups": ups, "downs": downs,
            "coff": np.array([[OWN * c]], dtype=np.uint32),
        }
        in_maps.append(m)
    return in_maps


def kernel(**inputs):
    T = int(np.asarray(inputs["L0_attn_x"]).shape[1])
    nc = _build(T)
    in_maps = _prep_inputs(inputs, T)
    res = run_bass_kernel_spmd(nc, in_maps, core_ids=list(range(NCORES)))
    out = np.concatenate([res.results[c]["out"] for c in range(NCORES)], axis=0)
    return out.reshape(1, T, D).astype(np.float32)
